# version 21
# baseline (speedup 1.0000x reference)
import numpy as np
import ml_dtypes
import concourse.bass as bass
import concourse.mybir as mybir
from concourse.bass_utils import run_bass_kernel_spmd
from concourse.alu_op_type import AluOpType as ALU

F32 = mybir.dt.float32
BF16 = mybir.dt.bfloat16
AF = mybir.ActivationFunctionType
AX = mybir.AxisListType.X

D = 1024
DCH = 8
NOWN = 2048
NFULL = 4096
HEADS = 16
NHP = 8
NT = 4
NTW = 512
MC = 32
NTILES = 32
EPS = 1e-5

ENGS = ("sync", "tensor", "vector", "scalar", "gpsimd")
SEMS = ("pe", "dve", "act", "xa", "xb", "xc", "xd", "wa", "wb", "wc", "we", "g", "zg", "zs", "zs2", "zs3", "zs4", "out", "ou2", "ou3", "ou4", "wd", "od", "zh", "pl", "x2r", "x2b", "x2c")
DMA_SEMS = {"xa", "xb", "g", "zg", "zs", "out"}


class Sched:

    def __init__(self, block, sems):
        self.block = block
        self.sems = sems
        self.q = {e: [] for e in ENGS}
        self.ctr = {s: 0 for s in SEMS}
        self.waited = {e: {s: 0 for s in SEMS} for e in ENGS}

    def emit(self, eng, fn, waits=(), inc=None, inc_n=1):
        w = []
        for s, v in waits:
            if v > self.waited[eng][s]:
                self.waited[eng][s] = v
                w.append((s, v))
        tgt = None
        if inc is not None:
            self.ctr[inc] += inc_n
            tgt = self.ctr[inc]
        self.q[eng].append((tuple(w), fn, inc, inc_n))
        return tgt

    def barrier_waits(self, eng):
        out = []
        for s in SEMS:
            if s == "pe" and eng == "tensor":
                continue
            if s == "dve" and eng == "vector":
                continue
            if s == "act" and eng == "scalar":
                continue
            if s == "pl" and eng == "gpsimd":
                continue
            if self.ctr[s]:
                out.append((s, self.ctr[s]))
        return out

    def flush(self):
        for eng in ENGS:
            items = self.q[eng]
            if not items:
                continue
            self.q[eng] = []

            def body(e, items=items):
                for waits, fn, inc, inc_n in items:
                    for s, v in waits:
                        e.wait_ge(self.sems[s], v)
                    r = fn(e)
                    if inc is not None:
                        r.then_inc(self.sems[inc], inc_n)

            getattr(self.block, eng)(body)


def build_nc(debug=False, upto='D'):
    import os
    PARANOID = bool(os.environ.get("KPARANOID"))
    KNT = int(os.environ.get("KNT", str(NT)))
    KHP = int(os.environ.get("KHP", str(NHP)))
    nc = bass.Bass("TRN2", target_bir_lowering=False)

    x_own = nc.dram_tensor("x_own", [NOWN, D], F32, kind="ExternalInput").ap()
    x_oth = nc.dram_tensor("x_oth", [NOWN, D], F32, kind="ExternalInput").ap()
    wq_d = nc.dram_tensor("wq", [D, D], F32, kind="ExternalInput").ap()
    wk_d = nc.dram_tensor("wk", [D, 64], F32, kind="ExternalInput").ap()
    wv_d = nc.dram_tensor("wv", [D, 64], F32, kind="ExternalInput").ap()
    wp_d = nc.dram_tensor("wp", [D, D], F32, kind="ExternalInput").ap()
    win2_d = nc.dram_tensor("win2", [D, 128], F32, kind="ExternalInput").ap()
    wout2_d = nc.dram_tensor("wout2", [128, D], F32, kind="ExternalInput").ap()
    w1p_d = nc.dram_tensor("w1p", [128, 8], F32, kind="ExternalInput").ap()
    b1p_d = nc.dram_tensor("b1p", [128, 8], F32, kind="ExternalInput").ap()
    w2p_d = nc.dram_tensor("w2p", [128, 8], F32, kind="ExternalInput").ap()
    bin2_d = nc.dram_tensor("bin2", [128, 1], F32, kind="ExternalInput").ap()
    b2h_d = nc.dram_tensor("b2h", [128, 1], F32, kind="ExternalInput").ap()
    boutbc_d = nc.dram_tensor("boutbc", [128, D], F32, kind="ExternalInput").ap()
    ident_d = nc.dram_tensor("ident", [128, 128], BF16, kind="ExternalInput").ap()
    out_d = nc.dram_tensor("out", [NOWN, D], F32, kind="ExternalOutput").ap()
    x2d = nc.dram_tensor("x2d", [NOWN, D], F32,
                         kind="ExternalOutput" if debug else "Internal").ap()
    zscr_h = nc.dram_tensor("zscr", [64, 16, NTW], F32, kind="Internal")
    zscr = zscr_h.ap()

    dbg = {}
    if debug:
        dbg["k2t"] = nc.dram_tensor("dbg_k2t", [128, NFULL], BF16, kind="ExternalOutput").ap()
        dbg["vab"] = nc.dram_tensor("dbg_vab", [128, MC * 65], BF16, kind="ExternalOutput").ap()
        dbg["qt"] = nc.dram_tensor("dbg_qt", [128, DCH * NOWN], BF16, kind="ExternalOutput").ap()
        dbg["ot"] = nc.dram_tensor("dbg_ot", [128, DCH * NOWN], BF16, kind="ExternalOutput").ap()
        dbg["tv"] = nc.dram_tensor("dbg_tv", [128, HEADS * NTW * NT], F32, kind="ExternalOutput").ap()
        dbg["r2t"] = nc.dram_tensor("dbg_r2t", [128, DCH * NOWN], BF16, kind="ExternalOutput").ap()

    x_tiles = [x_own.rearrange("(t p) d -> t p d", p=128),
               x_oth.rearrange("(t p) d -> t p d", p=128)]

    def xtile(i):
        return x_tiles[0][i] if i < 16 else x_tiles[1][i - 16]

    import contextlib
    st = contextlib.ExitStack()
    with st:
        xbuf = st.enter_context(nc.sbuf_tensor("xbuf", [128, 4, D], F32))
        big = st.enter_context(nc.sbuf_tensor("big", [128, 8192], F32))
        zstage = st.enter_context(nc.sbuf_tensor("zstage", [16, NTW], F32))
        recipz = st.enter_context(nc.sbuf_tensor("recipz", [16, NTW], F32))
        rcpb = st.enter_context(nc.sbuf_tensor("rcpb", [128, 4, NTW], F32))
        hfp = st.enter_context(nc.sbuf_tensor("hfp", [128, 2, NTW], F32))
        g2 = st.enter_context(nc.sbuf_tensor("g2", [128, 2, NTW], F32))
        boutbc = st.enter_context(nc.sbuf_tensor("boutbc_s", [128, D], F32))
        stats = st.enter_context(nc.sbuf_tensor("stats", [128, 8, 4], F32))
        w1p = st.enter_context(nc.sbuf_tensor("w1p_s", [128, 8], F32))
        b1p = st.enter_context(nc.sbuf_tensor("b1p_s", [128, 8], F32))
        b1adj = st.enter_context(nc.sbuf_tensor("b1adj", [128, 8], F32))
        ones1 = st.enter_context(nc.sbuf_tensor("ones1", [1, 128], F32))
        w2p = st.enter_context(nc.sbuf_tensor("w2p_s", [128, 8], F32))
        bin2 = st.enter_context(nc.sbuf_tensor("bin2_s", [128, 1], F32))
        b2h = st.enter_context(nc.sbuf_tensor("b2h_s", [128, 1], F32))
        epsb = st.enter_context(nc.sbuf_tensor("epsb", [128, 1], F32))
        wq_wp = st.enter_context(nc.sbuf_tensor("wq_wp", [128, DCH, D], BF16))
        wkb = st.enter_context(nc.sbuf_tensor("wkb", [128, DCH, 64], BF16))
        wvb = st.enter_context(nc.sbuf_tensor("wvb", [128, DCH, 64], BF16))
        win2b = st.enter_context(nc.sbuf_tensor("win2b", [128, DCH, 128], BF16))
        wout2b = st.enter_context(nc.sbuf_tensor("wout2b", [128, D], BF16))
        identb = st.enter_context(nc.sbuf_tensor("identb", [128, 128], BF16))
        k2t = st.enter_context(nc.sbuf_tensor("k2t", [128, NFULL], BF16))
        vab = st.enter_context(nc.sbuf_tensor("vab", [128, MC, 65], BF16))
        qt = st.enter_context(nc.sbuf_tensor("qt", [128, DCH, NOWN], BF16))
        ot = st.enter_context(nc.sbuf_tensor("ot", [128, DCH, NOWN], BF16))
        r1t = st.enter_context(nc.sbuf_tensor("r1t", [128, 2, DCH, NTW], BF16))
        ptu = st.enter_context(nc.sbuf_tensor("ptu", [128, 7, 1024], BF16))
        r1u = st.enter_context(nc.sbuf_tensor("r1u", [128, 2, D], BF16))
        g2b = st.enter_context(nc.sbuf_tensor("g2b", [128, NTW], BF16))

        sems = {s: st.enter_context(nc.semaphore("sem_" + s)) for s in SEMS}
        block = st.enter_context(nc.Block())
        S = Sched(block, sems)

        def tv(i):
            return big[:, i * NTW:(i + 1) * NTW]
        def x2buf(q):
            return big[:, 1024 + q * 1024: 1024 + (q + 1) * 1024]
        def x2dbuf(q):
            return big[:, 4096 + q * 1024: 4096 + (q + 1) * 1024]
        def obuf(q):
            return big[:, 6144 + q * 1024: 6144 + (q + 1) * 1024]

        stat = lambda k, sc: stats[:, k, sc:sc + 1]
        def dump(sl):
            return big[:, 0:1024]
        SUMS, SUMSQ, MEAN, EX2, M2, VAR, LNV, RSTD = range(8)

        ln_state = {"sq": {}, "sq2": {}, "rstd": {}, "i": 0}

        def ln_stage1(li, xt_ap, sc, sq_waits=()):
            dmp = dump(li % 2)
            ln_state["sqa" + str(li)] = S.emit(
                "scalar", lambda s, a=xt_ap, dmp=dmp, c=sc: s.activation(
                    dmp, a, AF.Square, accum_out=stat(SUMSQ, c)),
                waits=list(sq_waits) + [("act", ln_state["sq"].get(li - 1, 0))], inc="act")
            ln_state["sq"][li] = S.emit(
                "scalar", lambda s, a=xt_ap, dmp=dmp, c=sc: s.activation(
                    dmp, a, AF.Identity, accum_out=stat(SUMS, c)),
                waits=[("act", ln_state["sqa" + str(li)])], inc="act")

        def ln_stage2(li, sc):
            t1 = S.emit("scalar", lambda s, c=sc: s.activation(
                stat(M2, c), stat(SUMS, c), AF.Square, scale=1.0 / D),
                waits=[("act", ln_state["sq"][li])], inc="act")
            t2 = S.emit("scalar", lambda s, c=sc: s.activation(
                stat(VAR, c), stat(M2, c), AF.Identity, scale=-1.0, bias=epsb[:]),
                waits=[("act", t1)], inc="act")
            t3 = S.emit("scalar", lambda s, c=sc: s.activation(
                stat(LNV, c), stat(SUMSQ, c), AF.Ln, scale=1.0 / D, bias=stat(VAR, c)),
                waits=[("act", t2)], inc="act")
            t4 = S.emit("scalar", lambda s, c=sc: s.activation(
                stat(RSTD, c), stat(LNV, c), AF.Exp, scale=-0.5),
                waits=[("act", t3)], inc="act")
            ln_state["rstd"][li] = S.emit(
                "scalar", lambda s, c=sc: s.activation(
                    stat(MEAN, c), stat(SUMS, c), AF.Identity, scale=1.0 / D),
                waits=[("act", t4)], inc="act")

        def ln_stage3(li, xt_ap, sc, out_ap, extra_norm_waits=()):
            tnorm = S.emit("vector", lambda v, a=xt_ap, c=sc, o=out_ap: v.tensor_scalar(
                o, a, stat(MEAN, c), stat(RSTD, c), op0=ALU.subtract, op1=ALU.mult),
                waits=[("act", ln_state["rstd"][li])] +
                      list(extra_norm_waits), inc="dve")
            return tnorm

        def transpose8(src_ap_fn, trps, half, waits):
            t = None
            for jj in range(DCH):
                last = jj == DCH - 1
                t = S.emit("tensor", lambda te, jj=jj, h=half: te.transpose(
                    trps[:, h * 1024 + jj * 128: h * 1024 + (jj + 1) * 128],
                    src_ap_fn(jj), identb[:]),
                    waits=waits if jj == 0 else (), inc="pe" if last else None)
            return t

        wsem4 = ["xa", "xb", "xc", "xd"]
        wldsem = ["wa", "wb", "wc", "we"]
        wcast_t = {}

        dbg_ms = 0
        if debug:
            S.emit("vector", lambda v: v.memset(big[:], 0.0))
            dbg_ms = S.emit("vector", lambda v: v.memset(ot[:], 0.0), inc="dve")
        S.emit("vector", lambda v: v.memset(epsb[:], EPS))
        S.emit("vector", lambda v: v.memset(ones1[:], 1.0))
        tvone = S.emit("vector", lambda v: v.memset(vab[:, :, 64:65], 1.0), inc="dve")

        early_xload = [None]

        def wstg(idx):
            s = idx % 4
            return big[:, 1024 + s * 1024: 2048 + s * 1024]

        def load_cast(name, dram_ap, rows, cols, dst_ap, idx):
            sm = wldsem[idx % 4]
            lw = [("dve", wcast_t.get(idx - 4, dbg_ms))]
            tl = S.emit("sync", lambda e, d=dram_ap, rows=rows, cols=cols, idx=idx:
                        e.dma_start(wstg(idx)[0:rows, 0:cols], d),
                        waits=lw, inc=sm, inc_n=16)
            tc = S.emit("vector", lambda v, r=rows, c=cols, dst=dst_ap, idx=idx:
                        v.tensor_copy(dst, wstg(idx)[0:r, 0:c]),
                        waits=[(sm, tl)], inc="dve")
            wcast_t[idx] = tc
            return tc

        xload_t = {}
        for i0_ in range(4):
            xload_t[i0_] = S.emit("sync", lambda e, i=i0_: e.dma_start(
                xbuf[:, i % 4, :], xtile(i)), inc=wsem4[i0_ % 4], inc_n=16)

        idx = 0
        last_casts = []
        wq_r = wq_d.rearrange("(j p) hk -> j p hk", p=128)
        for j in range(DCH):
            last_casts.append(load_cast(f"wq{j}", wq_r[j], 128, D, wq_wp[:, j, :], idx)); idx += 1
        def load_cast3(name, dram3, cols, kdim, dst3, idx):
            sm = wldsem[idx % 4]
            stg = wstg(idx)[0:128, 0:cols].rearrange("p (j k) -> p j k", k=kdim)
            tl = S.emit("sync", lambda e, d=dram3, stg=stg: e.dma_start(stg, d),
                        waits=[("dve", wcast_t.get(idx - 4, dbg_ms))], inc=sm, inc_n=16)
            tc = S.emit("vector", lambda v, stg=stg, dst=dst3: v.tensor_copy(dst, stg),
                        waits=[(sm, tl)], inc="dve")
            wcast_t[idx] = tc
            return tc

        wk_r = wk_d.rearrange("(j p) k -> p j k", p=128)
        last_casts.append(load_cast3("wk", wk_r, 512, 64, wkb[:], idx)); idx += 1
        wv_r = wv_d.rearrange("(j p) k -> p j k", p=128)
        last_casts.append(load_cast3("wv", wv_r, 512, 64, wvb[:], idx)); idx += 1
        win_r = win2_d.rearrange("(j p) m -> p j m", p=128)
        last_casts.append(load_cast3("win", win_r, 1024, 128, win2b[:], idx)); idx += 1
        last_casts.append(load_cast("wout", wout2_d, 128, 1024, wout2b[:, :], idx)); idx += 1

        direct = [(w1p_d, w1p[:]), (b1p_d, b1p[:]), (w2p_d, w2p[:]), (bin2_d, bin2[:]),
                  (b2h_d, b2h[:]), (boutbc_d, boutbc[:]), (ident_d, identb[:])]
        tdl = None
        for dsrc, ddst in direct:
            tdl = S.emit("sync", lambda e, a=dsrc, b=ddst: e.dma_start(b, a), inc="wd", inc_n=16)
        t_ident = tdl
        tb1 = S.emit("vector", lambda v: v.scalar_tensor_tensor(
            b1adj[:], w1p[:], bin2[:], b1p[:], op0=ALU.mult, op1=ALU.add),
            waits=[("wd", tdl)], inc="dve")
        S.flush()

        pA = st.enter_context(contextlib.ExitStack())
        trpsA = pA.enter_context(nc.psum_tensor("trpsA", [128, 2048], BF16))
        kps = pA.enter_context(nc.psum_tensor("kps", [64, NTW], F32))
        vps = pA.enter_context(nc.psum_tensor("vps", [128, NTW], F32))
        qps = pA.enter_context(nc.psum_tensor("qps", [128, 2, NTW], F32))

        norm_t = {}
        trans_t = {}
        evac_t = {}
        dbg_war = {}
        slabmm_t = {}
        kdup_t = {}
        qevac_t = {}
        kevac_t = {}
        vevac_t = {}

        first_eng_waits = {e: S.barrier_waits(e) for e in ENGS}

        def emit_slab_mm(s, own):
            R1T = r1t[:, s % 2]
            i_last = 4 * s + 3
            mmw = [("dve", evac_t[i_last]), ("dve", kevac_t.get(s - 1, 0)),
                   ("dve", vevac_t.get(s - 1, 0)), ("dve", wcast_t[9])]
            t = None
            for j in range(DCH):
                t = S.emit("tensor", lambda te, j=j: te.matmul(
                    kps[:], wkb[:, j, :], R1T[:, j, :], start=(j == 0), stop=(j == DCH - 1)),
                    waits=mmw if j == 0 else (), inc="pe" if j == DCH - 1 else None)
            tK = t
            kevac_t[s] = S.emit("vector", lambda v, s=s: v.tensor_copy(
                k2t[0:64, s * 512:(s + 1) * 512], kps[:]), waits=[("pe", tK)], inc="dve")
            kdup_t[s] = S.emit("gpsimd", lambda e, s=s: e.dma_start(
                k2t[64:128, s * 512:(s + 1) * 512], k2t[0:64, s * 512:(s + 1) * 512]),
                waits=[("dve", kevac_t[s])] + (first_eng_waits["gpsimd"] if s == 0 else []),
                inc="g", inc_n=16)
            for sl in range(4):
                for j in range(DCH):
                    t = S.emit("tensor", lambda te, j=j, sl=sl: te.matmul(
                        vps[:, sl * 64:(sl + 1) * 64], R1T[:, j, sl * 128:(sl + 1) * 128],
                        wvb[:, j, :], start=(j == 0), stop=(j == DCH - 1)),
                        inc="pe" if (j == DCH - 1 and sl == 3) else None)
            tV = t
            vev = None
            for sl in range(4):
                vev = S.emit("vector", lambda v, s=s, sl=sl: v.tensor_copy(
                    vab[:, 4 * s + sl, 0:64], vps[:, sl * 64:(sl + 1) * 64]),
                    waits=[("pe", tV)], inc="dve" if sl == 3 else None)
            vevac_t[s] = vev
            if own:
                for c in range(DCH):
                    qw = [("dve", qevac_t.get((s, c - 2), 0))]
                    for j in range(DCH):
                        t = S.emit("tensor", lambda te, j=j, c=c: te.matmul(
                            qps[:, c % 2, :], wq_wp[:, j, c * 128:(c + 1) * 128], R1T[:, j, :],
                            start=(j == 0), stop=(j == DCH - 1)),
                            waits=qw if j == 0 else (), inc="pe" if j == DCH - 1 else None)
                    qevac_t[(s, c)] = S.emit("vector", lambda v, s=s, c=c: v.tensor_copy(
                        qt[:, c, s * 512:(s + 1) * 512], qps[:, c % 2, :]),
                        waits=[("pe", t)], inc="dve")
            slabmm_t[s] = t if own else tV

        def emit_xload_A(i):
            q = i % 4
            w = [("dve", norm_t.get(i - 4, 0))]
            xload_t[i] = S.emit("sync", lambda e, i=i, q=q: e.dma_start(xbuf[:, q, :], xtile(i)),
                                waits=w, inc=wsem4[q], inc_n=16)

        for step in range(NTILES + 3):
            i = step - 1
            if 0 <= i < NTILES:
                ln_stage2(i, i % 4)
            i = step
            if i < NTILES:
                q = i % 4
                sm = wsem4[q]
                ln_w = [(sm, xload_t[i])]
                if i == 0:
                    ln_w += [("dve", tvone)]
                ln_stage1(i, xbuf[:, q, :], i % 4, sq_waits=ln_w)
            i = step - 2
            if 0 <= i < NTILES:
                q, qr = i % 4, i % 2
                norm_t[i] = ln_stage3(i, xbuf[:, q, :], i % 4, r1u[:, qr, :],
                                      extra_norm_waits=[("pe", trans_t.get(i - 2, 0))])
            if step >= 2 and step + 2 < NTILES:
                emit_xload_A(step + 2)
            i = step - 2
            if 0 <= i < NTILES:
                tw = [("dve", norm_t[i]), ("dve", evac_t.get(i - 2, 0))]
                if i == 0:
                    tw += [("wd", t_ident)]
                trans_t[i] = transpose8(lambda jj, qr=qr: r1u[:, qr, jj * 128:(jj + 1) * 128],
                                        trpsA, qr, tw)
            i = step - 3
            if 0 <= i < NTILES:
                qr, tt_ = i % 2, i % 4
                evac_t[i] = S.emit("vector", lambda v, qr=qr, tt_=tt_, sb=(i // 4) % 2: v.tensor_copy(
                    r1t[:, sb, 0:DCH, tt_ * 128:(tt_ + 1) * 128],
                    trpsA[:, qr * 1024:(qr + 1) * 1024].rearrange("p (j t) -> p j t", t=128)),
                    waits=[("pe", trans_t[i]), ("pe", slabmm_t.get(i // 4 - 2, 0))], inc="dve")
                if i % 4 == 3:
                    emit_slab_mm(i // 4, i < 16)

        S.flush()
        pA.close()

        if debug:
            S.emit("gpsimd", lambda e: e.dma_start(dbg["k2t"][:], k2t[:]),
                   waits=S.barrier_waits("gpsimd"), inc="g", inc_n=16)
            S.emit("gpsimd", lambda e: e.dma_start(
                dbg["vab"][:], vab[:].rearrange("p m v -> p (m v)")), inc="g", inc_n=16)
            S.emit("gpsimd", lambda e: e.dma_start(
                dbg["qt"][:], qt[:].rearrange("p j n -> p (j n)")), inc="g", inc_n=16)
            S.flush()

        if upto == 'A':
            S.emit("gpsimd", lambda e: e.nop(), waits=S.barrier_waits("gpsimd"))
            S.flush()
            return nc

        pB = st.enter_context(contextlib.ExitStack())
        spa = pB.enter_context(nc.psum_tensor("spa", [128, 1024], F32))
        spb = pB.enter_context(nc.psum_tensor("spb", [128, 1024], F32))
        oAe = pB.enter_context(nc.psum_tensor("oAe", [65, NTW], F32))
        oAo = pB.enter_context(nc.psum_tensor("oAo", [65, NTW], F32))
        oBe = pB.enter_context(nc.psum_tensor("oBe", [65, NTW], F32))
        oBo = pB.enter_context(nc.psum_tensor("oBo", [65, NTW], F32))
        obank = [oAe, oAo, oBe, oBo]
        sp = [spa, spb]

        bwB = {e: S.barrier_waits(e) for e in ENGS}
        wpc_t = {}
        for j in range(DCH):
            sm = wldsem[j % 2]
            tl = S.emit("sync", lambda e, j=j: e.dma_start(
                xbuf[:, j % 2, :], wp_d.rearrange("(j p) d -> j p d", p=128)[j]),
                waits=(bwB["sync"] if j == 0 else []) + [("dve", wpc_t.get(j - 2, 0))],
                inc=sm, inc_n=16)
            wpc_t[j] = S.emit("vector", lambda v, j=j: v.tensor_copy(wq_wp[:, j, :], xbuf[:, j % 2, :]),
                              waits=[(sm, tl)] + (bwB["vector"] if j == 0 else []), inc="dve")

        dbgtv_t = {}
        zstate = {}

        def emit_zmult(k, i_):
            zst = zstate[k]
            c = i_ // 2
            if i_ % 2 == 0:
                dst = ot[0:64, c, k * NTW:(k + 1) * NTW]
            else:
                dst = r1u[0:64, c % 2, 0:NTW]
            mw = [(ZS[i_ % 4], zst["bcT"][i_]), ("dve", tcopyT[k * NHP + i_ // 2])]
            if i_ % 2 == 1:
                mw.append(("od", zst["odmaT"]))
            tm = S.emit("vector", lambda v, i_=i_, dst=dst: v.tensor_tensor(
                dst, tv(i_)[0:64, :], rcpb[0:64, i_ % 4, :], op=ALU.mult),
                waits=mw, inc="dve")
            multT[(k, i_)] = tm
            zst["tm"] = tm
            if i_ + 4 < HEADS:
                emit_bcast_ref[0](k, i_ + 4)
            if i_ % 2 == 1:
                zst["odmaT"] = S.emit("gpsimd", lambda e, i_=i_, c=c, k=k: e.dma_start(
                    ot[64:128, c, k * NTW:(k + 1) * NTW], r1u[0:64, c % 2, 0:NTW]),
                    waits=[("dve", tm)], inc="od", inc_n=16)
            if i_ == HEADS - 1:
                multT[k] = tm
                odma_fin[k] = zst["odmaT"]

        ZS = ("zs", "zs2", "zs3", "zs4")
        emit_bcast_ref = [None]
        odma_fin = {}
        trec0 = [0]
        expT = {}
        sPT = {}
        pvT = {}
        tcopyT = {}
        pvbank = {}
        tcfree = {}
        multT = {}
        gatherT = {}
        first_pe = True
        first_act = True

        NG = KNT * NHP * MC
        PVLAG = 5
        first_pe = True
        first_act = True

        def hp_of(g):
            return g // MC

        def s_pair(gmc):
            nonlocal first_pe
            k, j, mc = gmc // (NHP * MC), (gmc // MC) % NHP, gmc % MC
            w = [("act", expT.get(gmc - 2, 0))]
            if first_pe:
                w += bwB["tensor"]
                first_pe = False
            for hh in range(2):
                rg = hh * 64
                tgt = S.emit("tensor", lambda te, mc=mc, rg=rg, hh=hh, j=j, k=k: te.matmul(
                    sp[gmc % 2][:, hh * 512:(hh + 1) * 512],
                    k2t[rg:rg + 64, mc * 128:(mc + 1) * 128],
                    qt[rg:rg + 64, j, k * NTW:(k + 1) * NTW],
                    start=True, stop=True),
                    waits=w if hh == 0 else (), inc="pe" if hh == 1 else None)
            sPT[gmc] = tgt

        for it in range(-2, NG + PVLAG):
            gmc = it
            if 0 <= gmc < NG:
                ew = [("pe", sPT[gmc]), ("pe", pvT.get(gmc - 4, 0))]
                if first_act:
                    ew += bwB["scalar"]
                    first_act = False
                expT[gmc] = S.emit("scalar", lambda s_, gmc=gmc: s_.activation(
                    ptu[:, gmc % 7, :], sp[gmc % 2][:], AF.Exp, scale=0.125),
                    waits=ew, inc="act")
            gmc = it + 2
            if 0 <= gmc < NG:
                if gmc % MC == PVLAG + 6 and gmc // MC >= NHP and KHP == NHP:
                    jj = (gmc // MC) % NHP
                    kprev = (gmc // MC - NHP) // NHP
                    assert kprev in zstate, (gmc, kprev)
                    for hh_ in range(2):
                        emit_zmult(kprev, 2 * jj + hh_)
                s_pair(gmc)
            outs = [(oAe, 0, 0), (oAo, 64, 0), (oBe, 0, 512), (oBo, 64, 512)]
            for oi in range(4):
                gmc = it - PVLAG + oi
                if not (0 <= gmc < NG):
                    continue
                mc = gmc % MC
                hp = gmc // MC
                k, j = gmc // (NHP * MC), (gmc // MC) % NHP
                ob, rg, col = outs[oi]
                w_i = [("act", expT[gmc])]
                if mc == 0:
                    w_i.append(("dve", tcfree.get((hp - 1, oi), 0)))
                tgt = S.emit("tensor", lambda te, ob=ob, rg=rg, col=col, mc=mc, gmc=gmc: te.matmul(
                    ob[:], vab[rg:rg + 64, mc, :], ptu[rg:rg + 64, gmc % 7, col:col + 512],
                    start=(mc == 0), stop=(mc == MC - 1)),
                    waits=w_i, inc="pe" if (oi == 3 or mc == MC - 1) else None)
                if oi == 3:
                    pvT[gmc] = tgt
                if mc != MC - 1:
                    continue
                pvbank[(hp, oi)] = tgt
                hh = oi // 2
                hi = 2 * j + hh
                if oi % 2 == 1:
                    tw = [("pe", tgt), ("dve", multT.get((k - 1, hi), 0))]
                    if debug and j == 0:
                        tw.append(("g", dbgtv_t.get(k - 1, 0)))
                    tcfree[(hp, oi)] = S.emit("vector", lambda v, hi=hi, hh=hh: v.tensor_copy(
                        tv(hi)[0:65, :], obank[2 * hh + 1][:]), waits=tw, inc="dve")
                else:
                    tc = S.emit("vector", lambda v, hi=hi, hh=hh: v.tensor_tensor(
                        tv(hi)[0:65, :], tv(hi)[0:65, :], obank[2 * hh][:], op=ALU.add),
                        waits=[("dve", tcfree[(hp, oi + 1)]), ("pe", tgt)], inc="dve")
                    tcfree[(hp, oi)] = tc
                    tg = S.emit("gpsimd", lambda e, hi=hi: e.dma_start(
                        zstage[hi:hi + 1, :], tv(hi)[64:65, :]),
                        waits=[("dve", tc)] + (bwB["gpsimd"] if hp == 0 else []),
                        inc="zg", inc_n=16)
                    if oi == 0:
                        tcopyT[hp] = tc
            gmc = it - PVLAG
            if 0 <= gmc < NG:
                k, j, mc = gmc // (NHP * MC), (gmc // MC) % NHP, gmc % MC
                hp = gmc // MC
                if mc == MC - 1:
                    tg = None
                    if j == NHP - 1 and KHP == NHP and os.environ.get("KNTEND") != "0":
                        gatherT[k] = S.ctr["zg"]
                        zst = {"bcT": {}, "odmaT": S.ctr["od"], "tm": None}
                        zstate[k] = zst
                        if debug:
                            dbgtv_t[k] = S.emit("gpsimd", lambda e, k=k: e.dma_start(
                                dbg["tv"][:, k * 8192:(k + 1) * 8192], big[:, 0:8192]),
                                waits=[("dve", tcopyT[k * NHP + NHP - 1])], inc="g", inc_n=16)
                        rw = [("zg", gatherT[k])]
                        tzw = [("dve", trec0[0])]
                        if k > 0 and (k - 1) in zstate:
                            for s_ in range(4):
                                rw.append((ZS[s_], zstate[k - 1]["bcT"][12 + s_]))
                                tzw.append((ZS[s_], zstate[k - 1]["bcT"][12 + s_]))
                        trec = S.emit("vector", lambda v: v.reciprocal(recipz[:], zstage[:]),
                                      waits=rw, inc="dve")
                        trec0[0] = trec
                        tzw[0] = ("dve", trec)
                        tzo = None
                        for cc in range(4):
                            tzo = S.emit("sync", lambda e, cc=cc: e.dma_start(zscr[cc], recipz[:]),
                                         waits=tzw if cc == 0 else (), inc="zh", inc_n=16)
                        seed = tzo
                        for rr in range(1, 16):
                            tzo = S.emit("sync", lambda e, rr=rr: e.dma_start(
                                zscr[4 * rr:4 * rr + 4], zscr[0:4]),
                                waits=[("zh", seed)] if rr == 1 else (), inc="zh", inc_n=16)

                        def emit_bcast(kk, i_, tzo=tzo):
                            zst = zstate[kk]
                            bw = [("zh", tzo)]
                            if i_ >= 4:
                                bw.append(("dve", multT[(kk, i_ - 4)]))
                            else:
                                bw.append(("dve", multT.get((kk - 1, 12 + i_), 0)))
                            zst["bcT"][i_] = S.emit("gpsimd", lambda e, i_=i_: e.dma_start(
                                rcpb[0:64, i_ % 4, :],
                                zscr.rearrange("c h n -> h c n")[i_]),
                                waits=bw, inc=ZS[i_ % 4], inc_n=16)

                        emit_bcast_ref[0] = emit_bcast
                        for i_ in range(4):
                            emit_bcast(k, i_)

        S.flush()
        pB.close()



        if upto == 'B':
            S.emit("gpsimd", lambda e: e.nop(), waits=S.barrier_waits("gpsimd"))
            S.flush()
            return nc

        pC = st.enter_context(contextlib.ExitStack())
        op_ps = pC.enter_context(nc.psum_tensor("op_ps", [128, 2, 1024], F32))
        trpsC = pC.enter_context(nc.psum_tensor("trpsC", [128, 1024], BF16))

        s2ps = pC.enter_context(nc.psum_tensor("s2ps", [128, NTW], F32))
        fo = pC.enter_context(nc.psum_tensor("fo", [128, 1024], F32))

        bwC = {e: S.barrier_waits(e) for e in ENGS}
        x2add_t = {}
        trans_tC = {}
        r2evac_t = {}
        tfa_t = {}
        out_r = out_d.rearrange("(t p) d -> t p d", p=128)
        x2d_r = x2d.rearrange("(t p) d -> t p d", p=128)
        spill_t = {}
        r2norm_t = {}
        relu_t = {}
        gacc_t = {}
        fadd_t = {}
        outdma_t = {}
        fo_mm_t = {}
        LB = 100

        def emit_D_head(k):
            sw = [("act", relu_t.get((k - 1, 7), 0)), ("dve", r2evac_t[4 * k + 3])]
            tmm = None
            for j in range(DCH):
                tmm = S.emit("tensor", lambda te, k=k, j=j: te.matmul(
                    s2ps[:], win2b[:, j, :], qt[:, j, k * NTW:(k + 1) * NTW],
                    start=(j == 0), stop=(j == DCH - 1)),
                    waits=sw if j == 0 else (), inc="pe" if j == DCH - 1 else None)
            for p in range(8):
                rw = [("pe", tmm), ("dve", gacc_t.get(k * 8 + p - 2, 0))]
                relu_t[(k, p)] = S.emit("scalar", lambda s_, k=k, p=p: s_.activation(
                    hfp[:, p % 2, :], s2ps[:], AF.Relu,
                    bias=b1adj[:, p:p + 1], scale=w1p[:, p:p + 1]),
                    waits=rw, inc="act")
                gw = [("act", relu_t[(k, p)])]
                if p == 0:
                    gacc_t[k * 8 + p] = S.emit("vector", lambda v: v.tensor_scalar(
                        g2[:, 0, :], hfp[:, 0, :], w2p[:, 0:1], b2h[:],
                        op0=ALU.mult, op1=ALU.add), waits=gw, inc="dve")
                else:
                    gacc_t[k * 8 + p] = S.emit("vector", lambda v, p=p: v.scalar_tensor_tensor(
                        g2[:, p % 2, :], hfp[:, p % 2, :], w2p[:, p:p + 1], g2[:, (p + 1) % 2, :],
                        op0=ALU.mult, op1=ALU.add),
                        waits=gw + [("dve", gacc_t[k * 8 + p - 1])], inc="dve")
            tg2b_t[k] = S.emit("vector", lambda v: v.tensor_copy(g2b[:], g2[:, 1, :]),
                               waits=[("dve", gacc_t[k * 8 + 7]),
                                      ("pe", fo_mm_t.get(4 * k - 1, 0))], inc="dve")

        def emit_D_ns(k, ns):
            if True:
                t = k * 4 + ns
                q = ns % 2
                fw = [("dve", tg2b_t[k]), ("dve", tfa_t.get(t - 1, 0))]
                for half in range(2):
                    S.emit("tensor", lambda te, ns=ns, half=half: te.matmul(
                        fo[:, half * 512:(half + 1) * 512], g2b[:, ns * 128:(ns + 1) * 128],
                        wout2b[:, half * 512:(half + 1) * 512], start=True, stop=False),
                        waits=fw if half == 0 else ())
                    fo_mm_t[t] = S.emit("tensor", lambda te, half=half: te.matmul(
                        fo[:, half * 512:(half + 1) * 512], ones1[:],
                        boutbc[0:1, half * 512:(half + 1) * 512], start=False, stop=True),
                        inc="pe" if half == 1 else None)
                rsems = ("wd", "x2b") if t % 2 == 0 else ("x2r", "x2c")
                osems = ("out", "ou3") if t % 2 == 0 else ("ou2", "ou4")
                xw = [("g", spill_t[t]), ("dve", tfa_t.get(t - 2, 0)),
                      ("dve", multT.get((KL, HEADS - 1), 0))]
                txl = {}
                for h in range(2):
                    txl[h] = S.emit("sync", lambda e, t=t, q=q, h=h: e.dma_start(
                        x2dbuf(q)[:, h * 512:(h + 1) * 512],
                        x2d_r[t, :, h * 512:(h + 1) * 512]),
                        waits=xw if h == 0 else (), inc=rsems[h], inc_n=16)
                tfa = S.emit("vector", lambda v, q=q: v.tensor_tensor(
                    obuf(q), x2dbuf(q), fo[:], op=ALU.add),
                    waits=[("pe", fo_mm_t[t]), (rsems[0], txl[0]), (rsems[1], txl[1]),
                           (osems[0], outdma_t.get((t - 2, 0), 0)),
                           (osems[1], outdma_t.get((t - 2, 1), 0)),
                           ("dve", multT.get((KL, HEADS - 1), 0))],
                    inc="dve")
                tfa_t[t] = tfa
                for h in range(2):
                    outdma_t[(t, h)] = S.emit("sync", lambda e, t=t, q=q, h=h: e.dma_start(
                        out_r[t, :, h * 512:(h + 1) * 512], obuf(q)[:, h * 512:(h + 1) * 512]),
                        waits=[("dve", tfa)] if h == 0 else (), inc=osems[h], inc_n=16)

        tg2b_t = {}
        KL = KNT - 1
        for step in range(16 + 8):
            if step < 4 and KL in zstate and KHP == NHP:
                for i_ in range(4 * step, 4 * step + 4):
                    emit_zmult(KL, i_)
            if step == 4 and debug and KL in zstate:
                S.emit("gpsimd", lambda e: e.dma_start(
                    dbg["ot"][:], ot[:].rearrange("p j n -> p (j n)")),
                    waits=[("dve", multT[(KL, HEADS - 1)]), ("od", odma_fin[KL])],
                    inc="zg", inc_n=16)
            t = step - 2
            if 0 <= t < 16:
                ln_stage2(LB + t, t % 4)
            t = step - 1
            if 0 <= t < 16:
                q2 = t % 3
                sm = wsem4[t % 4]
                aw = [("pe", trans_tC[('mm', t)]), (sm, xload_t[('C', t)]),
                      ("g", spill_t.get(t - 3, 0)), ("dve", r2norm_t.get(t - 3, 0)),
                      ("dve", multT.get((KL, 3 + 2 * (t % 3)), 0))]
                if t == 0:
                    aw += bwC["vector"]
                x2add_t[t] = S.emit("vector", lambda v, t=t, q2=q2: v.tensor_tensor(
                    x2buf(q2), xbuf[:, t % 4, :], op_ps[:, t % 2, :], op=ALU.add),
                    waits=aw, inc="dve")
                ln_stage1(LB + t, x2buf(q2), t % 4,
                          sq_waits=[("dve", x2add_t[t]), ("dve", multT.get((KL, 1), 0))] +
                                   ([("g", dbgtv_t.get(KL, 0))] if debug else []) +
                                   (bwC["scalar"] if t == 0 else []))
            for t in ([0, 1] if step == 0 else []) + ([step + 2] if step + 2 < 16 else []):
                q = t % 4
                xw = [("dve", x2add_t.get(t - 4, 0))]
                if t == 0:
                    xw += bwC["sync"]
                xload_t[('C', t)] = S.emit("sync", lambda e, t=t, q=q: e.dma_start(
                    xbuf[:, q, :], x_tiles[0][t]), waits=xw, inc=wsem4[q], inc_n=16)
            for t in ([0] if step == 0 else []) + [step + 1]:
                if t >= 16:
                    continue
                q = t % 4
                sm = wsem4[q]
                mw = [("dve", x2add_t.get(t - 2, 0)),
                      ("dve", multT.get((t // 4, HEADS - 1), 0)),
                      ("od", odma_fin.get(t // 4, 0))]
                if t == 0:
                    mw += bwC["tensor"]
                tmm = None
                for half in range(2):
                    for c in range(DCH):
                        first = half == 0 and c == 0
                        tmm = S.emit("tensor", lambda te, t=t, half=half, c=c: te.matmul(
                            op_ps[:, t % 2, half * 512:(half + 1) * 512],
                            ot[:, c, t * 128:(t + 1) * 128],
                            wq_wp[:, c, half * 512:(half + 1) * 512],
                            start=(c == 0), stop=(c == DCH - 1)),
                            waits=mw if first else (), inc="pe" if (half == 1 and c == DCH - 1) else None)
                trans_tC[('mm', t)] = tmm
            t = step - 8
            if 0 <= t < 16:
                emit_D_ns(t // 4, t % 4)
            t = step - 4
            if 0 <= t < 16:
                r2evac_t[t] = S.emit("vector", lambda v, t=t: v.tensor_copy(
                    qt[:, 0:DCH, t * 128:(t + 1) * 128],
                    trpsC[:, 0:1024].rearrange("p (j t) -> p j t", t=128)),
                    waits=[("pe", trans_tC[t])], inc="dve")
                if t % 4 == 3:
                    emit_D_head(t // 4)
            t = step - 3
            if 0 <= t < 16:
                q2, qr = t % 3, t % 2
                r2norm_t[t] = ln_stage3(LB + t, x2buf(q2), t % 4, ptu[:, qr, :],
                                        extra_norm_waits=[("pe", trans_tC.get(t - 2, 0))])
                spill_t[t] = S.emit("gpsimd", lambda e, t=t, q2=q2: e.dma_start(
                    x2d.rearrange("(t p) d -> t p d", p=128)[t], x2buf(q2)),
                    waits=[("dve", r2norm_t[t]), ("g", spill_t.get(t - 1, 0))] +
                          (bwC["gpsimd"] if t == 0 else []), inc="g", inc_n=16)
                tw = [("dve", r2norm_t[t]), ("dve", r2evac_t.get(t - 1, 0))]
                trans_tC[t] = transpose8(lambda jj, qr=qr: ptu[:, qr, jj * 128:(jj + 1) * 128],
                                         trpsC, 0, tw)

        if debug:
            S.emit("gpsimd", lambda e: e.dma_start(
                dbg["r2t"][:], qt[:].rearrange("p j n -> p (j n)")),
                waits=S.barrier_waits("gpsimd"), inc="g", inc_n=16)

        S.emit("gpsimd", lambda e: e.nop(), waits=[
            ("out", outdma_t[(14, 0)]), ("ou3", outdma_t[(14, 1)]),
            ("ou2", outdma_t[(15, 0)]), ("ou4", outdma_t[(15, 1)])])
        S.flush()
        pC.close()

    return nc


_NC_CACHE = {}


def _get_nc(debug=False):
    if debug not in _NC_CACHE:
        _NC_CACHE[debug] = build_nc(debug)
    return _NC_CACHE[debug]


def host_prep(inputs):
    f = np.float32
    x = np.asarray(inputs["L0_attn_x"], f)
    Wq = np.asarray(inputs["L0_attn_query_proj"], f)
    Wk = np.asarray(inputs["L0_attn_key_proj"], f)
    Wv = np.asarray(inputs["L0_attn_value_proj"], f)
    Wp = np.asarray(inputs["L0_attn_output_proj"], f)
    Win = np.asarray(inputs["L0_ffn_Win"], f)
    bin_ = np.asarray(inputs["L0_ffn_bin_"], f)
    W1 = np.asarray(inputs["L0_ffn_W1"], f)
    b1 = np.asarray(inputs["L0_ffn_b1"], f)
    W2 = np.asarray(inputs["L0_ffn_W2"], f)
    b2 = np.asarray(inputs["L0_ffn_b2"], f)
    Wout = np.asarray(inputs["L0_ffn_Wout"], f)
    bout = np.asarray(inputs["L0_ffn_bout"], f)

    common = {
        "wq": np.ascontiguousarray(Wq.transpose(2, 0, 1).reshape(D, D)),
        "wk": np.ascontiguousarray(Wk),
        "wv": np.ascontiguousarray(Wv),
        "wp": np.ascontiguousarray(Wp.transpose(1, 2, 0).reshape(D, D)),
        "win2": np.ascontiguousarray(np.concatenate([Win, Win], axis=1)),
        "wout2": np.ascontiguousarray(np.concatenate([Wout, Wout], axis=0)),
        "w1p": np.ascontiguousarray(np.concatenate([W1[:, 0::2], W1[:, 1::2]], axis=0)),
        "b1p": np.ascontiguousarray(np.concatenate([b1[:, 0::2], b1[:, 1::2]], axis=0)),
        "w2p": np.ascontiguousarray(np.concatenate([W2[:, 0::2], W2[:, 1::2]], axis=0)),
        "bin2": np.ascontiguousarray(np.concatenate([bin_, bin_])[:, None]),
        "b2h": np.ascontiguousarray(np.concatenate([b2, np.zeros(64, f)])[:, None]),
        "boutbc": np.ascontiguousarray(np.broadcast_to(bout, (128, D))),
        "ident": np.eye(128, dtype=ml_dtypes.bfloat16),
    }
    in_maps = []
    for c in range(8):
        b, h = c // 2, c % 2
        m = dict(common)
        m["x_own"] = np.ascontiguousarray(x[b, h * NOWN:(h + 1) * NOWN])
        m["x_oth"] = np.ascontiguousarray(x[b, (1 - h) * NOWN:(2 - h) * NOWN])
        in_maps.append(m)
    return in_maps


def kernel(**inputs):
    in_maps = host_prep(inputs)
    nc = _get_nc(debug=False)
    res = run_bass_kernel_spmd(nc, in_maps, core_ids=list(range(8)))
    out = np.empty((4, 4096, 1024), np.float32)
    for c in range(8):
        b, h = c // 2, c % 2
        out[b, h * NOWN:(h + 1) * NOWN] = np.asarray(res.results[c]["out"])
    return out


# version 22 (final)
# speedup vs baseline: 1.0419x; 1.0419x over previous
import numpy as np
import ml_dtypes
import concourse.bass as bass
import concourse.mybir as mybir
from concourse.bass_utils import run_bass_kernel_spmd
from concourse.alu_op_type import AluOpType as ALU

F32 = mybir.dt.float32
BF16 = mybir.dt.bfloat16
AF = mybir.ActivationFunctionType
AX = mybir.AxisListType.X

D = 1024
DCH = 8
NOWN = 2048
NFULL = 4096
HEADS = 16
NHP = 8
NT = 4
NTW = 512
MC = 32
NTILES = 32
EPS = 1e-5

ENGS = ("sync", "tensor", "vector", "scalar", "gpsimd")
SEMS = ("pe", "dve", "act", "xa", "xb", "xc", "xd", "wa", "wb", "wc", "we", "g", "zg", "zs", "zs2", "zs3", "zs4", "out", "ou2", "ou3", "ou4", "wd", "od", "zh", "pl", "x2r", "x2b", "x2c")
DMA_SEMS = {"xa", "xb", "g", "zg", "zs", "out"}


class Sched:

    def __init__(self, block, sems):
        self.block = block
        self.sems = sems
        self.q = {e: [] for e in ENGS}
        self.ctr = {s: 0 for s in SEMS}
        self.waited = {e: {s: 0 for s in SEMS} for e in ENGS}

    def emit(self, eng, fn, waits=(), inc=None, inc_n=1):
        w = []
        for s, v in waits:
            if v > self.waited[eng][s]:
                self.waited[eng][s] = v
                w.append((s, v))
        tgt = None
        if inc is not None:
            self.ctr[inc] += inc_n
            tgt = self.ctr[inc]
        self.q[eng].append((tuple(w), fn, inc, inc_n))
        return tgt

    def barrier_waits(self, eng):
        out = []
        for s in SEMS:
            if s == "pe" and eng == "tensor":
                continue
            if s == "dve" and eng == "vector":
                continue
            if s == "act" and eng == "scalar":
                continue
            if s == "pl" and eng == "gpsimd":
                continue
            if self.ctr[s]:
                out.append((s, self.ctr[s]))
        return out

    def flush(self):
        for eng in ENGS:
            items = self.q[eng]
            if not items:
                continue
            self.q[eng] = []

            def body(e, items=items):
                for waits, fn, inc, inc_n in items:
                    for s, v in waits:
                        e.wait_ge(self.sems[s], v)
                    r = fn(e)
                    if inc is not None:
                        r.then_inc(self.sems[inc], inc_n)

            getattr(self.block, eng)(body)


def build_nc(debug=False, upto='D'):
    import os
    PARANOID = bool(os.environ.get("KPARANOID"))
    KNT = int(os.environ.get("KNT", str(NT)))
    KHP = int(os.environ.get("KHP", str(NHP)))
    nc = bass.Bass("TRN2", target_bir_lowering=False)

    x_own = nc.dram_tensor("x_own", [NOWN, D], F32, kind="ExternalInput").ap()
    x_oth = nc.dram_tensor("x_oth", [NOWN, D], F32, kind="ExternalInput").ap()
    wq_d = nc.dram_tensor("wq", [D, D], F32, kind="ExternalInput").ap()
    wk_d = nc.dram_tensor("wk", [D, 64], F32, kind="ExternalInput").ap()
    wv_d = nc.dram_tensor("wv", [D, 64], F32, kind="ExternalInput").ap()
    wp_d = nc.dram_tensor("wp", [D, D], F32, kind="ExternalInput").ap()
    win2_d = nc.dram_tensor("win2", [D, 128], F32, kind="ExternalInput").ap()
    wout2_d = nc.dram_tensor("wout2", [128, D], F32, kind="ExternalInput").ap()
    w1p_d = nc.dram_tensor("w1p", [128, 8], F32, kind="ExternalInput").ap()
    b1p_d = nc.dram_tensor("b1p", [128, 8], F32, kind="ExternalInput").ap()
    w2p_d = nc.dram_tensor("w2p", [128, 8], F32, kind="ExternalInput").ap()
    bin2_d = nc.dram_tensor("bin2", [128, 1], F32, kind="ExternalInput").ap()
    b2h_d = nc.dram_tensor("b2h", [128, 1], F32, kind="ExternalInput").ap()
    boutbc_d = nc.dram_tensor("boutbc", [128, D], F32, kind="ExternalInput").ap()
    ident_d = nc.dram_tensor("ident", [128, 128], BF16, kind="ExternalInput").ap()
    out_d = nc.dram_tensor("out", [NOWN, D], F32, kind="ExternalOutput").ap()
    x2d = nc.dram_tensor("x2d", [NOWN, D], F32,
                         kind="ExternalOutput" if debug else "Internal").ap()
    zscr_h = nc.dram_tensor("zscr", [64, 16, NTW], F32, kind="Internal")
    zscr = zscr_h.ap()

    dbg = {}
    if debug:
        dbg["k2t"] = nc.dram_tensor("dbg_k2t", [128, NFULL], BF16, kind="ExternalOutput").ap()
        dbg["vab"] = nc.dram_tensor("dbg_vab", [128, MC * 65], BF16, kind="ExternalOutput").ap()
        dbg["qt"] = nc.dram_tensor("dbg_qt", [128, DCH * NOWN], BF16, kind="ExternalOutput").ap()
        dbg["ot"] = nc.dram_tensor("dbg_ot", [128, DCH * NOWN], BF16, kind="ExternalOutput").ap()
        dbg["tv"] = nc.dram_tensor("dbg_tv", [128, HEADS * NTW * NT], F32, kind="ExternalOutput").ap()
        dbg["r2t"] = nc.dram_tensor("dbg_r2t", [128, DCH * NOWN], BF16, kind="ExternalOutput").ap()

    x_tiles = [x_own.rearrange("(t p) d -> t p d", p=128),
               x_oth.rearrange("(t p) d -> t p d", p=128)]

    def xtile(i):
        return x_tiles[0][i] if i < 16 else x_tiles[1][i - 16]

    import contextlib
    st = contextlib.ExitStack()
    with st:
        xbuf = st.enter_context(nc.sbuf_tensor("xbuf", [128, 4, D], F32))
        big = st.enter_context(nc.sbuf_tensor("big", [128, 8192], F32))
        zstage = st.enter_context(nc.sbuf_tensor("zstage", [16, NTW], F32))
        recipz = st.enter_context(nc.sbuf_tensor("recipz", [16, NTW], F32))
        rcpb = st.enter_context(nc.sbuf_tensor("rcpb", [128, 4, NTW], F32))
        hfp = st.enter_context(nc.sbuf_tensor("hfp", [128, 2, NTW], F32))
        g2 = st.enter_context(nc.sbuf_tensor("g2", [128, 2, NTW], F32))
        boutbc = st.enter_context(nc.sbuf_tensor("boutbc_s", [128, D], F32))
        stats = st.enter_context(nc.sbuf_tensor("stats", [128, 8, 4], F32))
        w1p = st.enter_context(nc.sbuf_tensor("w1p_s", [128, 8], F32))
        b1p = st.enter_context(nc.sbuf_tensor("b1p_s", [128, 8], F32))
        b1adj = st.enter_context(nc.sbuf_tensor("b1adj", [128, 8], F32))
        ones1 = st.enter_context(nc.sbuf_tensor("ones1", [1, 128], F32))
        w2p = st.enter_context(nc.sbuf_tensor("w2p_s", [128, 8], F32))
        bin2 = st.enter_context(nc.sbuf_tensor("bin2_s", [128, 1], F32))
        b2h = st.enter_context(nc.sbuf_tensor("b2h_s", [128, 1], F32))
        epsb = st.enter_context(nc.sbuf_tensor("epsb", [128, 1], F32))
        wq_wp = st.enter_context(nc.sbuf_tensor("wq_wp", [128, DCH, D], BF16))
        wkb = st.enter_context(nc.sbuf_tensor("wkb", [128, DCH, 64], BF16))
        wvb = st.enter_context(nc.sbuf_tensor("wvb", [128, DCH, 64], BF16))
        win2b = st.enter_context(nc.sbuf_tensor("win2b", [128, DCH, 128], BF16))
        wout2b = st.enter_context(nc.sbuf_tensor("wout2b", [128, D], BF16))
        identb = st.enter_context(nc.sbuf_tensor("identb", [128, 128], BF16))
        k2t = st.enter_context(nc.sbuf_tensor("k2t", [128, NFULL], BF16))
        vab = st.enter_context(nc.sbuf_tensor("vab", [128, MC, 65], BF16))
        qt = st.enter_context(nc.sbuf_tensor("qt", [128, DCH, NOWN], BF16))
        ot = st.enter_context(nc.sbuf_tensor("ot", [128, DCH, NOWN], BF16))
        r1t = st.enter_context(nc.sbuf_tensor("r1t", [128, 2, DCH, NTW], BF16))
        ptu = st.enter_context(nc.sbuf_tensor("ptu", [128, 7, 1024], BF16))
        r1u = st.enter_context(nc.sbuf_tensor("r1u", [128, 2, D], BF16))
        g2b = st.enter_context(nc.sbuf_tensor("g2b", [128, NTW], BF16))

        sems = {s: st.enter_context(nc.semaphore("sem_" + s)) for s in SEMS}
        block = st.enter_context(nc.Block())
        S = Sched(block, sems)

        def tv(i):
            return big[:, i * NTW:(i + 1) * NTW]
        def x2buf(q):
            return big[:, 1024 + q * 1024: 1024 + (q + 1) * 1024]
        def x2dbuf(q):
            return big[:, 4096 + q * 1024: 4096 + (q + 1) * 1024]
        def obuf(q):
            return big[:, 6144 + q * 1024: 6144 + (q + 1) * 1024]

        stat = lambda k, sc: stats[:, k, sc:sc + 1]
        def dump(sl):
            return big[:, 0:1024]
        SUMS, SUMSQ, MEAN, EX2, M2, VAR, LNV, RSTD = range(8)

        ln_state = {"sq": {}, "sq2": {}, "rstd": {}, "i": 0}

        def ln_stage1(li, xt_ap, sc, sq_waits=()):
            dmp = dump(li % 2)
            ln_state["sqa" + str(li)] = S.emit(
                "scalar", lambda s, a=xt_ap, dmp=dmp, c=sc: s.activation(
                    dmp, a, AF.Square, accum_out=stat(SUMSQ, c)),
                waits=list(sq_waits) + [("act", ln_state["sq"].get(li - 1, 0))], inc="act")
            ln_state["sq"][li] = S.emit(
                "scalar", lambda s, a=xt_ap, dmp=dmp, c=sc: s.activation(
                    dmp, a, AF.Identity, accum_out=stat(SUMS, c)),
                waits=[("act", ln_state["sqa" + str(li)])], inc="act")

        def ln_stage2(li, sc):
            t1 = S.emit("scalar", lambda s, c=sc: s.activation(
                stat(M2, c), stat(SUMS, c), AF.Square, scale=1.0 / D),
                waits=[("act", ln_state["sq"][li])], inc="act")
            t2 = S.emit("scalar", lambda s, c=sc: s.activation(
                stat(VAR, c), stat(M2, c), AF.Identity, scale=-1.0, bias=epsb[:]),
                waits=[("act", t1)], inc="act")
            t3 = S.emit("scalar", lambda s, c=sc: s.activation(
                stat(LNV, c), stat(SUMSQ, c), AF.Ln, scale=1.0 / D, bias=stat(VAR, c)),
                waits=[("act", t2)], inc="act")
            t4 = S.emit("scalar", lambda s, c=sc: s.activation(
                stat(RSTD, c), stat(LNV, c), AF.Exp, scale=-0.5),
                waits=[("act", t3)], inc="act")
            ln_state["rstd"][li] = S.emit(
                "scalar", lambda s, c=sc: s.activation(
                    stat(MEAN, c), stat(SUMS, c), AF.Identity, scale=1.0 / D),
                waits=[("act", t4)], inc="act")

        def ln_stage3(li, xt_ap, sc, out_ap, extra_norm_waits=()):
            tnorm = S.emit("vector", lambda v, a=xt_ap, c=sc, o=out_ap: v.tensor_scalar(
                o, a, stat(MEAN, c), stat(RSTD, c), op0=ALU.subtract, op1=ALU.mult),
                waits=[("act", ln_state["rstd"][li])] +
                      list(extra_norm_waits), inc="dve")
            return tnorm

        def transpose8(src_ap_fn, trps, half, waits):
            t = None
            for jj in range(DCH):
                last = jj == DCH - 1
                t = S.emit("tensor", lambda te, jj=jj, h=half: te.transpose(
                    trps[:, h * 1024 + jj * 128: h * 1024 + (jj + 1) * 128],
                    src_ap_fn(jj), identb[:]),
                    waits=waits if jj == 0 else (), inc="pe" if last else None)
            return t

        wsem4 = ["xa", "xb", "xc", "xd"]
        wldsem = ["wa", "wb", "wc", "we"]
        wcast_t = {}

        dbg_ms = 0
        if debug:
            S.emit("vector", lambda v: v.memset(big[:], 0.0))
            dbg_ms = S.emit("vector", lambda v: v.memset(ot[:], 0.0), inc="dve")
        S.emit("vector", lambda v: v.memset(epsb[:], EPS))
        S.emit("vector", lambda v: v.memset(ones1[:], 1.0))
        tvone = S.emit("vector", lambda v: v.memset(vab[:, :, 64:65], 1.0), inc="dve")

        early_xload = [None]

        def wstg(idx):
            s = idx % 4
            return big[:, 1024 + s * 1024: 2048 + s * 1024]

        def load_cast(name, dram_ap, rows, cols, dst_ap, idx):
            sm = wldsem[idx % 4]
            lw = [("dve", wcast_t.get(idx - 4, dbg_ms))]
            tl = S.emit("sync", lambda e, d=dram_ap, rows=rows, cols=cols, idx=idx:
                        e.dma_start(wstg(idx)[0:rows, 0:cols], d),
                        waits=lw, inc=sm, inc_n=16)
            tc = S.emit("vector", lambda v, r=rows, c=cols, dst=dst_ap, idx=idx:
                        v.tensor_copy(dst, wstg(idx)[0:r, 0:c]),
                        waits=[(sm, tl)], inc="dve")
            wcast_t[idx] = tc
            return tc

        xload_t = {}
        for i0_ in range(4):
            xload_t[i0_] = S.emit("sync", lambda e, i=i0_: e.dma_start(
                xbuf[:, i % 4, :], xtile(i)), inc=wsem4[i0_ % 4], inc_n=16)

        idx = 0
        last_casts = []
        wq_r = wq_d.rearrange("(j p) hk -> j p hk", p=128)
        for j in range(DCH):
            last_casts.append(load_cast(f"wq{j}", wq_r[j], 128, D, wq_wp[:, j, :], idx)); idx += 1
        def load_cast3(name, dram3, cols, kdim, dst3, idx):
            sm = wldsem[idx % 4]
            stg = wstg(idx)[0:128, 0:cols].rearrange("p (j k) -> p j k", k=kdim)
            tl = S.emit("sync", lambda e, d=dram3, stg=stg: e.dma_start(stg, d),
                        waits=[("dve", wcast_t.get(idx - 4, dbg_ms))], inc=sm, inc_n=16)
            tc = S.emit("vector", lambda v, stg=stg, dst=dst3: v.tensor_copy(dst, stg),
                        waits=[(sm, tl)], inc="dve")
            wcast_t[idx] = tc
            return tc

        wk_r = wk_d.rearrange("(j p) k -> p j k", p=128)
        last_casts.append(load_cast3("wk", wk_r, 512, 64, wkb[:], idx)); idx += 1
        wv_r = wv_d.rearrange("(j p) k -> p j k", p=128)
        last_casts.append(load_cast3("wv", wv_r, 512, 64, wvb[:], idx)); idx += 1
        win_r = win2_d.rearrange("(j p) m -> p j m", p=128)
        last_casts.append(load_cast3("win", win_r, 1024, 128, win2b[:], idx)); idx += 1
        last_casts.append(load_cast("wout", wout2_d, 128, 1024, wout2b[:, :], idx)); idx += 1

        direct = [(w1p_d, w1p[:]), (b1p_d, b1p[:]), (w2p_d, w2p[:]), (bin2_d, bin2[:]),
                  (b2h_d, b2h[:]), (boutbc_d, boutbc[:]), (ident_d, identb[:])]
        tdl = None
        for dsrc, ddst in direct:
            tdl = S.emit("sync", lambda e, a=dsrc, b=ddst: e.dma_start(b, a), inc="wd", inc_n=16)
        t_ident = tdl
        tb1 = S.emit("vector", lambda v: v.scalar_tensor_tensor(
            b1adj[:], w1p[:], bin2[:], b1p[:], op0=ALU.mult, op1=ALU.add),
            waits=[("wd", tdl)], inc="dve")
        S.flush()

        pA = st.enter_context(contextlib.ExitStack())
        trpsA = pA.enter_context(nc.psum_tensor("trpsA", [128, 2048], BF16))
        kps = pA.enter_context(nc.psum_tensor("kps", [64, NTW], F32))
        vps = pA.enter_context(nc.psum_tensor("vps", [128, NTW], F32))
        qps = pA.enter_context(nc.psum_tensor("qps", [128, 2, NTW], F32))

        norm_t = {}
        trans_t = {}
        evac_t = {}
        dbg_war = {}
        slabmm_t = {}
        kdup_t = {}
        qevac_t = {}
        kevac_t = {}
        vevac_t = {}

        first_eng_waits = {e: S.barrier_waits(e) for e in ENGS}

        def emit_slab_mm(s, own):
            R1T = r1t[:, s % 2]
            i_last = 4 * s + 3
            mmw = [("dve", evac_t[i_last]), ("dve", kevac_t.get(s - 1, 0)),
                   ("dve", vevac_t.get(s - 1, 0)), ("dve", wcast_t[9])]
            t = None
            for j in range(DCH):
                t = S.emit("tensor", lambda te, j=j: te.matmul(
                    kps[:], wkb[:, j, :], R1T[:, j, :], start=(j == 0), stop=(j == DCH - 1)),
                    waits=mmw if j == 0 else (), inc="pe" if j == DCH - 1 else None)
            tK = t
            kevac_t[s] = S.emit("vector", lambda v, s=s: v.tensor_copy(
                k2t[0:64, s * 512:(s + 1) * 512], kps[:]), waits=[("pe", tK)], inc="dve")
            kdup_t[s] = S.emit("gpsimd", lambda e, s=s: e.dma_start(
                k2t[64:128, s * 512:(s + 1) * 512], k2t[0:64, s * 512:(s + 1) * 512]),
                waits=[("dve", kevac_t[s])] + (first_eng_waits["gpsimd"] if s == 0 else []),
                inc="g", inc_n=16)
            for sl in range(4):
                for j in range(DCH):
                    t = S.emit("tensor", lambda te, j=j, sl=sl: te.matmul(
                        vps[:, sl * 64:(sl + 1) * 64], R1T[:, j, sl * 128:(sl + 1) * 128],
                        wvb[:, j, :], start=(j == 0), stop=(j == DCH - 1)),
                        inc="pe" if (j == DCH - 1 and sl == 3) else None)
            tV = t
            vev = None
            for sl in range(4):
                vev = S.emit("vector", lambda v, s=s, sl=sl: v.tensor_copy(
                    vab[:, 4 * s + sl, 0:64], vps[:, sl * 64:(sl + 1) * 64]),
                    waits=[("pe", tV)], inc="dve" if sl == 3 else None)
            vevac_t[s] = vev
            if own:
                for c in range(DCH):
                    qw = [("dve", qevac_t.get((s, c - 2), 0))]
                    for j in range(DCH):
                        t = S.emit("tensor", lambda te, j=j, c=c: te.matmul(
                            qps[:, c % 2, :], wq_wp[:, j, c * 128:(c + 1) * 128], R1T[:, j, :],
                            start=(j == 0), stop=(j == DCH - 1)),
                            waits=qw if j == 0 else (), inc="pe" if j == DCH - 1 else None)
                    qevac_t[(s, c)] = S.emit("vector", lambda v, s=s, c=c: v.tensor_copy(
                        qt[:, c, s * 512:(s + 1) * 512], qps[:, c % 2, :]),
                        waits=[("pe", t)], inc="dve")
            slabmm_t[s] = t if own else tV

        def emit_xload_A(i):
            q = i % 4
            w = [("dve", norm_t.get(i - 4, 0))]
            xload_t[i] = S.emit("sync", lambda e, i=i, q=q: e.dma_start(xbuf[:, q, :], xtile(i)),
                                waits=w, inc=wsem4[q], inc_n=16)

        for step in range(NTILES + 3):
            i = step - 1
            if 0 <= i < NTILES:
                ln_stage2(i, i % 4)
            i = step
            if i < NTILES:
                q = i % 4
                sm = wsem4[q]
                ln_w = [(sm, xload_t[i])]
                if i == 0:
                    ln_w += [("dve", tvone)]
                ln_stage1(i, xbuf[:, q, :], i % 4, sq_waits=ln_w)
            i = step - 2
            if 0 <= i < NTILES:
                q, qr = i % 4, i % 2
                norm_t[i] = ln_stage3(i, xbuf[:, q, :], i % 4, r1u[:, qr, :],
                                      extra_norm_waits=[("pe", trans_t.get(i - 2, 0))])
            if step >= 2 and step + 2 < NTILES:
                emit_xload_A(step + 2)
            i = step - 2
            if 0 <= i < NTILES:
                tw = [("dve", norm_t[i]), ("dve", evac_t.get(i - 2, 0))]
                if i == 0:
                    tw += [("wd", t_ident)]
                trans_t[i] = transpose8(lambda jj, qr=qr: r1u[:, qr, jj * 128:(jj + 1) * 128],
                                        trpsA, qr, tw)
            i = step - 3
            if 0 <= i < NTILES:
                qr, tt_ = i % 2, i % 4
                evac_t[i] = S.emit("vector", lambda v, qr=qr, tt_=tt_, sb=(i // 4) % 2: v.tensor_copy(
                    r1t[:, sb, 0:DCH, tt_ * 128:(tt_ + 1) * 128],
                    trpsA[:, qr * 1024:(qr + 1) * 1024].rearrange("p (j t) -> p j t", t=128)),
                    waits=[("pe", trans_t[i]), ("pe", slabmm_t.get(i // 4 - 2, 0))], inc="dve")
                if i % 4 == 3:
                    emit_slab_mm(i // 4, i < 16)

        S.flush()
        pA.close()

        if debug:
            S.emit("gpsimd", lambda e: e.dma_start(dbg["k2t"][:], k2t[:]),
                   waits=S.barrier_waits("gpsimd"), inc="g", inc_n=16)
            S.emit("gpsimd", lambda e: e.dma_start(
                dbg["vab"][:], vab[:].rearrange("p m v -> p (m v)")), inc="g", inc_n=16)
            S.emit("gpsimd", lambda e: e.dma_start(
                dbg["qt"][:], qt[:].rearrange("p j n -> p (j n)")), inc="g", inc_n=16)
            S.flush()

        if upto == 'A':
            S.emit("gpsimd", lambda e: e.nop(), waits=S.barrier_waits("gpsimd"))
            S.flush()
            return nc

        pB = st.enter_context(contextlib.ExitStack())
        spa = pB.enter_context(nc.psum_tensor("spa", [128, 1024], F32))
        spb = pB.enter_context(nc.psum_tensor("spb", [128, 1024], F32))
        oAe = pB.enter_context(nc.psum_tensor("oAe", [65, NTW], F32))
        oAo = pB.enter_context(nc.psum_tensor("oAo", [65, NTW], F32))
        oBe = pB.enter_context(nc.psum_tensor("oBe", [65, NTW], F32))
        oBo = pB.enter_context(nc.psum_tensor("oBo", [65, NTW], F32))
        obank = [oAe, oAo, oBe, oBo]
        sp = [spa, spb]

        bwB = {e: S.barrier_waits(e) for e in ENGS}
        wpc_t = {}
        for j in range(DCH):
            sm = wldsem[j % 2]
            tl = S.emit("sync", lambda e, j=j: e.dma_start(
                xbuf[:, j % 2, :], wp_d.rearrange("(j p) d -> j p d", p=128)[j]),
                waits=(bwB["sync"] if j == 0 else []) + [("dve", wpc_t.get(j - 2, 0))],
                inc=sm, inc_n=16)
            wpc_t[j] = S.emit("vector", lambda v, j=j: v.tensor_copy(wq_wp[:, j, :], xbuf[:, j % 2, :]),
                              waits=[(sm, tl)] + (bwB["vector"] if j == 0 else []), inc="dve")

        dbgtv_t = {}
        zstate = {}

        def emit_zmult(k, i_):
            zst = zstate[k]
            c = i_ // 2
            if i_ % 2 == 0:
                dst = ot[0:64, c, k * NTW:(k + 1) * NTW]
            else:
                dst = r1u[0:64, c % 2, 0:NTW]
            mw = [(ZS[i_ % 4], zst["bcT"][i_]), ("dve", tcopyT[k * NHP + i_ // 2])]
            if i_ % 2 == 1:
                mw.append(("od", zst["odmaT"]))
            tm = S.emit("vector", lambda v, i_=i_, dst=dst: v.tensor_tensor(
                dst, tv(i_)[0:64, :], rcpb[0:64, i_ % 4, :], op=ALU.mult),
                waits=mw, inc="dve")
            multT[(k, i_)] = tm
            zst["tm"] = tm
            if i_ + 4 < HEADS:
                emit_bcast_ref[0](k, i_ + 4)
            if i_ % 2 == 1:
                zst["odmaT"] = S.emit("gpsimd", lambda e, i_=i_, c=c, k=k: e.dma_start(
                    ot[64:128, c, k * NTW:(k + 1) * NTW], r1u[0:64, c % 2, 0:NTW]),
                    waits=[("dve", tm)], inc="od", inc_n=16)
            if i_ == HEADS - 1:
                multT[k] = tm
                odma_fin[k] = zst["odmaT"]

        ZS = ("zs", "zs2", "zs3", "zs4")
        split_state = {}
        emit_bcast_ref = [None]
        odma_fin = {}
        trec0 = [0]
        expT = {}
        sPT = {}
        pvT = {}
        tcopyT = {}
        pvbank = {}
        tcfree = {}
        multT = {}
        gatherT = {}
        first_pe = True
        first_act = True

        NG = KNT * NHP * MC
        PVLAG = 5
        first_pe = True
        first_act = True

        def hp_of(g):
            return g // MC

        def s_pair(gmc):
            nonlocal first_pe
            k, j, mc = gmc // (NHP * MC), (gmc // MC) % NHP, gmc % MC
            w = [("act", expT.get(gmc - 2, 0))]
            if first_pe:
                w += bwB["tensor"]
                first_pe = False
            for hh in range(2):
                rg = hh * 64
                tgt = S.emit("tensor", lambda te, mc=mc, rg=rg, hh=hh, j=j, k=k: te.matmul(
                    sp[gmc % 2][:, hh * 512:(hh + 1) * 512],
                    k2t[rg:rg + 64, mc * 128:(mc + 1) * 128],
                    qt[rg:rg + 64, j, k * NTW:(k + 1) * NTW],
                    start=True, stop=True),
                    waits=w if hh == 0 else (), inc="pe" if hh == 1 else None)
            sPT[gmc] = tgt

        for it in range(-2, NG + PVLAG):
            gmc = it
            if 0 <= gmc < NG:
                ew = [("pe", sPT[gmc]), ("pe", pvT.get(gmc - 4, 0))]
                if first_act:
                    ew += bwB["scalar"]
                    first_act = False
                expT[gmc] = S.emit("scalar", lambda s_, gmc=gmc: s_.activation(
                    ptu[:, gmc % 7, :], sp[gmc % 2][:], AF.Exp, scale=0.125),
                    waits=ew, inc="act")
            gmc = it + 2
            if 0 <= gmc < NG:
                if gmc % MC == PVLAG + 6 and gmc // MC >= NHP and KHP == NHP:
                    jj = (gmc // MC) % NHP
                    kprev = (gmc // MC - NHP) // NHP
                    assert kprev in zstate, (gmc, kprev)
                    for hh_ in range(2):
                        emit_zmult(kprev, 2 * jj + hh_)
                s_pair(gmc)
            outs = [(oAe, 0, 0), (oAo, 64, 0), (oBe, 0, 512), (oBo, 64, 512)]
            for oi in range(4):
                gmc = it - PVLAG + oi
                if not (0 <= gmc < NG):
                    continue
                mc = gmc % MC
                hp = gmc // MC
                k, j = gmc // (NHP * MC), (gmc // MC) % NHP
                ob, rg, col = outs[oi]
                w_i = [("act", expT[gmc])]
                if mc == 0:
                    w_i.append(("dve", tcfree.get((hp - 1, oi), 0)))
                tgt = S.emit("tensor", lambda te, ob=ob, rg=rg, col=col, mc=mc, gmc=gmc: te.matmul(
                    ob[:], vab[rg:rg + 64, mc, :], ptu[rg:rg + 64, gmc % 7, col:col + 512],
                    start=(mc == 0), stop=(mc == MC - 1)),
                    waits=w_i, inc="pe" if (oi == 3 or mc == MC - 1) else None)
                if oi == 3:
                    pvT[gmc] = tgt
                if mc != MC - 1:
                    continue
                pvbank[(hp, oi)] = tgt
                hh = oi // 2
                hi = 2 * j + hh
                if oi % 2 == 1:
                    tw = [("pe", tgt), ("dve", multT.get((k - 1, hi), 0))]
                    if debug and j == 0:
                        tw.append(("g", dbgtv_t.get(k - 1, 0)))
                    tcfree[(hp, oi)] = S.emit("vector", lambda v, hi=hi, hh=hh: v.tensor_copy(
                        tv(hi)[0:65, :], obank[2 * hh + 1][:]), waits=tw, inc="dve")
                else:
                    tc = S.emit("vector", lambda v, hi=hi, hh=hh: v.tensor_tensor(
                        tv(hi)[0:65, :], tv(hi)[0:65, :], obank[2 * hh][:], op=ALU.add),
                        waits=[("dve", tcfree[(hp, oi + 1)]), ("pe", tgt)], inc="dve")
                    tcfree[(hp, oi)] = tc
                    tg = S.emit("gpsimd", lambda e, hi=hi: e.dma_start(
                        zstage[hi:hi + 1, :], tv(hi)[64:65, :]),
                        waits=[("dve", tc)] + (bwB["gpsimd"] if hp == 0 else []),
                        inc="zg", inc_n=16)
                    if oi == 0:
                        tcopyT[hp] = tc
                        if (j == NHP - 2 and k == KNT - 1 and KHP == NHP
                                and os.environ.get("KNTEND") != "0"):
                            rwA = [("zg", S.ctr["zg"])]
                            if k > 0 and (k - 1) in zstate:
                                for s_ in range(4):
                                    rwA.append((ZS[s_], zstate[k - 1]["bcT"][12 + s_]))
                            trecA = S.emit("vector", lambda v: v.reciprocal(
                                recipz[0:14, :], zstage[0:14, :]), waits=rwA, inc="dve")
                            tzo = None
                            for cc in range(4):
                                tzo = S.emit("sync", lambda e, cc=cc: e.dma_start(
                                    zscr[cc, 0:14, :], recipz[0:14, :]),
                                    waits=([("dve", trecA)] + rwA[1:]) if cc == 0 else (),
                                    inc="zh", inc_n=16)
                            seedA = tzo
                            for rr in range(1, 16):
                                tzo = S.emit("sync", lambda e, rr=rr: e.dma_start(
                                    zscr[4 * rr:4 * rr + 4, 0:14, :], zscr[0:4, 0:14, :]),
                                    waits=[("zh", seedA)] if rr == 1 else (), inc="zh", inc_n=16)
                            split_state["tzoA"] = tzo
                            split_state["trecA"] = trecA
            gmc = it - PVLAG
            if 0 <= gmc < NG:
                k, j, mc = gmc // (NHP * MC), (gmc // MC) % NHP, gmc % MC
                hp = gmc // MC
                if mc == MC - 1:
                    tg = None
                    if j == NHP - 1 and KHP == NHP and os.environ.get("KNTEND") != "0":
                        gatherT[k] = S.ctr["zg"]
                        zst = {"bcT": {}, "odmaT": S.ctr["od"], "tm": None}
                        zstate[k] = zst
                        if debug:
                            dbgtv_t[k] = S.emit("gpsimd", lambda e, k=k: e.dma_start(
                                dbg["tv"][:, k * 8192:(k + 1) * 8192], big[:, 0:8192]),
                                waits=[("dve", tcopyT[k * NHP + NHP - 1])], inc="g", inc_n=16)
                        split_here = "tzoA" in split_state and k == KNT - 1
                        rw = [("zg", gatherT[k])]
                        tzw = [("dve", trec0[0])]
                        if k > 0 and (k - 1) in zstate:
                            for s_ in range(4):
                                rw.append((ZS[s_], zstate[k - 1]["bcT"][12 + s_]))
                                tzw.append((ZS[s_], zstate[k - 1]["bcT"][12 + s_]))
                        if not split_here:
                            trec = S.emit("vector", lambda v: v.reciprocal(recipz[:], zstage[:]),
                                          waits=rw, inc="dve")
                            trec0[0] = trec
                            tzw[0] = ("dve", trec)
                        if split_here:
                            trecB = S.emit("vector", lambda v: v.reciprocal(
                                recipz[:], zstage[:]),
                                waits=[("zg", gatherT[k]), ("dve", split_state["trecA"]),
                                       ("zh", split_state["tzoA"])],
                                inc="dve")
                            trec0[0] = trecB
                            tzo = None
                            for cc in range(4):
                                tzo = S.emit("sync", lambda e, cc=cc: e.dma_start(
                                    zscr[cc, 14:16, :], recipz[14:16, :]),
                                    waits=[("dve", trecB)] if cc == 0 else (),
                                    inc="pl", inc_n=16)
                            seedB = tzo
                            for rr in range(1, 16):
                                tzo = S.emit("sync", lambda e, rr=rr: e.dma_start(
                                    zscr[4 * rr:4 * rr + 4, 14:16, :], zscr[0:4, 14:16, :]),
                                    waits=[("pl", seedB)] if rr == 1 else (), inc="pl", inc_n=16)
                            btarg = [("zh", split_state["tzoA"])] * 14 + [("pl", tzo)] * 2
                        else:
                            tzo = None
                            for cc in range(4):
                                tzo = S.emit("sync", lambda e, cc=cc: e.dma_start(zscr[cc], recipz[:]),
                                             waits=tzw if cc == 0 else (), inc="zh", inc_n=16)
                            seed = tzo
                            for rr in range(1, 16):
                                tzo = S.emit("sync", lambda e, rr=rr: e.dma_start(
                                    zscr[4 * rr:4 * rr + 4], zscr[0:4]),
                                    waits=[("zh", seed)] if rr == 1 else (), inc="zh", inc_n=16)
                            btarg = [("zh", tzo)] * 16

                        def emit_bcast(kk, i_, btarg=btarg):
                            zst = zstate[kk]
                            bw = [btarg[i_]]
                            if i_ >= 4:
                                bw.append(("dve", multT[(kk, i_ - 4)]))
                            else:
                                bw.append(("dve", multT.get((kk - 1, 12 + i_), 0)))
                            zst["bcT"][i_] = S.emit("gpsimd", lambda e, i_=i_: e.dma_start(
                                rcpb[0:64, i_ % 4, :],
                                zscr.rearrange("c h n -> h c n")[i_]),
                                waits=bw, inc=ZS[i_ % 4], inc_n=16)

                        emit_bcast_ref[0] = emit_bcast
                        for i_ in range(4):
                            emit_bcast(k, i_)

        S.flush()
        pB.close()



        if upto == 'B':
            S.emit("gpsimd", lambda e: e.nop(), waits=S.barrier_waits("gpsimd"))
            S.flush()
            return nc

        pC = st.enter_context(contextlib.ExitStack())
        op_ps = pC.enter_context(nc.psum_tensor("op_ps", [128, 2, 1024], F32))
        trpsC = pC.enter_context(nc.psum_tensor("trpsC", [128, 1024], BF16))

        s2ps = pC.enter_context(nc.psum_tensor("s2ps", [128, NTW], F32))
        fo = pC.enter_context(nc.psum_tensor("fo", [128, 1024], F32))

        bwC = {e: S.barrier_waits(e) for e in ENGS}
        x2add_t = {}
        trans_tC = {}
        r2evac_t = {}
        tfa_t = {}
        out_r = out_d.rearrange("(t p) d -> t p d", p=128)
        x2d_r = x2d.rearrange("(t p) d -> t p d", p=128)
        spill_t = {}
        r2norm_t = {}
        relu_t = {}
        gacc_t = {}
        fadd_t = {}
        outdma_t = {}
        fo_mm_t = {}
        LB = 100

        def emit_D_head(k):
            sw = [("act", relu_t.get((k - 1, 7), 0)), ("dve", r2evac_t[4 * k + 3])]
            tmm = None
            for j in range(DCH):
                tmm = S.emit("tensor", lambda te, k=k, j=j: te.matmul(
                    s2ps[:], win2b[:, j, :], qt[:, j, k * NTW:(k + 1) * NTW],
                    start=(j == 0), stop=(j == DCH - 1)),
                    waits=sw if j == 0 else (), inc="pe" if j == DCH - 1 else None)
            for p in range(8):
                rw = [("pe", tmm), ("dve", gacc_t.get(k * 8 + p - 2, 0))]
                relu_t[(k, p)] = S.emit("scalar", lambda s_, k=k, p=p: s_.activation(
                    hfp[:, p % 2, :], s2ps[:], AF.Relu,
                    bias=b1adj[:, p:p + 1], scale=w1p[:, p:p + 1]),
                    waits=rw, inc="act")
                gw = [("act", relu_t[(k, p)])]
                if p == 0:
                    gacc_t[k * 8 + p] = S.emit("vector", lambda v: v.tensor_scalar(
                        g2[:, 0, :], hfp[:, 0, :], w2p[:, 0:1], b2h[:],
                        op0=ALU.mult, op1=ALU.add), waits=gw, inc="dve")
                else:
                    gacc_t[k * 8 + p] = S.emit("vector", lambda v, p=p: v.scalar_tensor_tensor(
                        g2[:, p % 2, :], hfp[:, p % 2, :], w2p[:, p:p + 1], g2[:, (p + 1) % 2, :],
                        op0=ALU.mult, op1=ALU.add),
                        waits=gw + [("dve", gacc_t[k * 8 + p - 1])], inc="dve")
            tg2b_t[k] = S.emit("vector", lambda v: v.tensor_copy(g2b[:], g2[:, 1, :]),
                               waits=[("dve", gacc_t[k * 8 + 7]),
                                      ("pe", fo_mm_t.get(4 * k - 1, 0))], inc="dve")

        def emit_D_ns(k, ns):
            if True:
                t = k * 4 + ns
                q = ns % 2
                fw = [("dve", tg2b_t[k]), ("dve", tfa_t.get(t - 1, 0))]
                for half in range(2):
                    S.emit("tensor", lambda te, ns=ns, half=half: te.matmul(
                        fo[:, half * 512:(half + 1) * 512], g2b[:, ns * 128:(ns + 1) * 128],
                        wout2b[:, half * 512:(half + 1) * 512], start=True, stop=False),
                        waits=fw if half == 0 else ())
                    fo_mm_t[t] = S.emit("tensor", lambda te, half=half: te.matmul(
                        fo[:, half * 512:(half + 1) * 512], ones1[:],
                        boutbc[0:1, half * 512:(half + 1) * 512], start=False, stop=True),
                        inc="pe" if half == 1 else None)
                rsems = ("wd", "x2b") if t % 2 == 0 else ("x2r", "x2c")
                osems = ("out", "ou3") if t % 2 == 0 else ("ou2", "ou4")
                xw = [("g", spill_t[t]), ("dve", tfa_t.get(t - 2, 0)),
                      ("dve", multT.get((KL, HEADS - 1), 0))]
                txl = {}
                for h in range(2):
                    txl[h] = S.emit("sync", lambda e, t=t, q=q, h=h: e.dma_start(
                        x2dbuf(q)[:, h * 512:(h + 1) * 512],
                        x2d_r[t, :, h * 512:(h + 1) * 512]),
                        waits=xw if h == 0 else (), inc=rsems[h], inc_n=16)
                tfa = S.emit("vector", lambda v, q=q: v.tensor_tensor(
                    obuf(q), x2dbuf(q), fo[:], op=ALU.add),
                    waits=[("pe", fo_mm_t[t]), (rsems[0], txl[0]), (rsems[1], txl[1]),
                           (osems[0], outdma_t.get((t - 2, 0), 0)),
                           (osems[1], outdma_t.get((t - 2, 1), 0)),
                           ("dve", multT.get((KL, HEADS - 1), 0))],
                    inc="dve")
                tfa_t[t] = tfa
                for h in range(2):
                    outdma_t[(t, h)] = S.emit("sync", lambda e, t=t, q=q, h=h: e.dma_start(
                        out_r[t, :, h * 512:(h + 1) * 512], obuf(q)[:, h * 512:(h + 1) * 512]),
                        waits=[("dve", tfa)] if h == 0 else (), inc=osems[h], inc_n=16)

        tg2b_t = {}
        KL = KNT - 1
        for step in range(16 + 8):
            if step < 4 and KL in zstate and KHP == NHP:
                for i_ in range(4 * step, 4 * step + 4):
                    emit_zmult(KL, i_)
            if step == 4 and debug and KL in zstate:
                S.emit("gpsimd", lambda e: e.dma_start(
                    dbg["ot"][:], ot[:].rearrange("p j n -> p (j n)")),
                    waits=[("dve", multT[(KL, HEADS - 1)]), ("od", odma_fin[KL])],
                    inc="zg", inc_n=16)
            t = step - 2
            if 0 <= t < 16:
                ln_stage2(LB + t, t % 4)
            t = step - 1
            if 0 <= t < 16:
                q2 = t % 3
                sm = wsem4[t % 4]
                aw = [("pe", trans_tC[('mm', t)]), (sm, xload_t[('C', t)]),
                      ("g", spill_t.get(t - 3, 0)), ("dve", r2norm_t.get(t - 3, 0)),
                      ("dve", multT.get((KL, 3 + 2 * (t % 3)), 0))]
                if t == 0:
                    aw += bwC["vector"]
                x2add_t[t] = S.emit("vector", lambda v, t=t, q2=q2: v.tensor_tensor(
                    x2buf(q2), xbuf[:, t % 4, :], op_ps[:, t % 2, :], op=ALU.add),
                    waits=aw, inc="dve")
                ln_stage1(LB + t, x2buf(q2), t % 4,
                          sq_waits=[("dve", x2add_t[t]), ("dve", multT.get((KL, 1), 0))] +
                                   ([("g", dbgtv_t.get(KL, 0))] if debug else []) +
                                   (bwC["scalar"] if t == 0 else []))
            for t in ([0, 1] if step == 0 else []) + ([step + 2] if step + 2 < 16 else []):
                q = t % 4
                xw = [("dve", x2add_t.get(t - 4, 0))]
                if t == 0:
                    xw += bwC["sync"]
                xload_t[('C', t)] = S.emit("sync", lambda e, t=t, q=q: e.dma_start(
                    xbuf[:, q, :], x_tiles[0][t]), waits=xw, inc=wsem4[q], inc_n=16)
            t = step
            if t < 16:
                q = t % 4
                sm = wsem4[q]
                mw = [("dve", x2add_t.get(t - 2, 0)),
                      ("dve", multT.get((t // 4, HEADS - 1), 0)),
                      ("od", odma_fin.get(t // 4, 0))]
                if t == 0:
                    mw += bwC["tensor"]
                tmm = None
                for half in range(2):
                    for c in range(DCH):
                        first = half == 0 and c == 0
                        tmm = S.emit("tensor", lambda te, t=t, half=half, c=c: te.matmul(
                            op_ps[:, t % 2, half * 512:(half + 1) * 512],
                            ot[:, c, t * 128:(t + 1) * 128],
                            wq_wp[:, c, half * 512:(half + 1) * 512],
                            start=(c == 0), stop=(c == DCH - 1)),
                            waits=mw if first else (), inc="pe" if (half == 1 and c == DCH - 1) else None)
                trans_tC[('mm', t)] = tmm
            t = step - 8
            if 0 <= t < 16:
                emit_D_ns(t // 4, t % 4)
            t = step - 4
            if 0 <= t < 16:
                r2evac_t[t] = S.emit("vector", lambda v, t=t: v.tensor_copy(
                    qt[:, 0:DCH, t * 128:(t + 1) * 128],
                    trpsC[:, 0:1024].rearrange("p (j t) -> p j t", t=128)),
                    waits=[("pe", trans_tC[t])], inc="dve")
                if t % 4 == 3:
                    emit_D_head(t // 4)
            t = step - 3
            if 0 <= t < 16:
                q2, qr = t % 3, t % 2
                r2norm_t[t] = ln_stage3(LB + t, x2buf(q2), t % 4, ptu[:, qr, :],
                                        extra_norm_waits=[("pe", trans_tC.get(t - 2, 0))])
                spill_t[t] = S.emit("gpsimd", lambda e, t=t, q2=q2: e.dma_start(
                    x2d.rearrange("(t p) d -> t p d", p=128)[t], x2buf(q2)),
                    waits=[("dve", r2norm_t[t]), ("g", spill_t.get(t - 1, 0))] +
                          (bwC["gpsimd"] if t == 0 else []), inc="g", inc_n=16)
                tw = [("dve", r2norm_t[t]), ("dve", r2evac_t.get(t - 1, 0))]
                trans_tC[t] = transpose8(lambda jj, qr=qr: ptu[:, qr, jj * 128:(jj + 1) * 128],
                                         trpsC, 0, tw)

        if debug:
            S.emit("gpsimd", lambda e: e.dma_start(
                dbg["r2t"][:], qt[:].rearrange("p j n -> p (j n)")),
                waits=S.barrier_waits("gpsimd"), inc="g", inc_n=16)

        S.emit("gpsimd", lambda e: e.nop(), waits=[
            ("out", outdma_t[(14, 0)]), ("ou3", outdma_t[(14, 1)]),
            ("ou2", outdma_t[(15, 0)]), ("ou4", outdma_t[(15, 1)])])
        S.flush()
        pC.close()

    return nc


_NC_CACHE = {}


def _get_nc(debug=False):
    if debug not in _NC_CACHE:
        _NC_CACHE[debug] = build_nc(debug)
    return _NC_CACHE[debug]


def host_prep(inputs):
    f = np.float32
    x = np.asarray(inputs["L0_attn_x"], f)
    Wq = np.asarray(inputs["L0_attn_query_proj"], f)
    Wk = np.asarray(inputs["L0_attn_key_proj"], f)
    Wv = np.asarray(inputs["L0_attn_value_proj"], f)
    Wp = np.asarray(inputs["L0_attn_output_proj"], f)
    Win = np.asarray(inputs["L0_ffn_Win"], f)
    bin_ = np.asarray(inputs["L0_ffn_bin_"], f)
    W1 = np.asarray(inputs["L0_ffn_W1"], f)
    b1 = np.asarray(inputs["L0_ffn_b1"], f)
    W2 = np.asarray(inputs["L0_ffn_W2"], f)
    b2 = np.asarray(inputs["L0_ffn_b2"], f)
    Wout = np.asarray(inputs["L0_ffn_Wout"], f)
    bout = np.asarray(inputs["L0_ffn_bout"], f)

    common = {
        "wq": np.ascontiguousarray(Wq.transpose(2, 0, 1).reshape(D, D)),
        "wk": np.ascontiguousarray(Wk),
        "wv": np.ascontiguousarray(Wv),
        "wp": np.ascontiguousarray(Wp.transpose(1, 2, 0).reshape(D, D)),
        "win2": np.ascontiguousarray(np.concatenate([Win, Win], axis=1)),
        "wout2": np.ascontiguousarray(np.concatenate([Wout, Wout], axis=0)),
        "w1p": np.ascontiguousarray(np.concatenate([W1[:, 0::2], W1[:, 1::2]], axis=0)),
        "b1p": np.ascontiguousarray(np.concatenate([b1[:, 0::2], b1[:, 1::2]], axis=0)),
        "w2p": np.ascontiguousarray(np.concatenate([W2[:, 0::2], W2[:, 1::2]], axis=0)),
        "bin2": np.ascontiguousarray(np.concatenate([bin_, bin_])[:, None]),
        "b2h": np.ascontiguousarray(np.concatenate([b2, np.zeros(64, f)])[:, None]),
        "boutbc": np.ascontiguousarray(np.broadcast_to(bout, (128, D))),
        "ident": np.eye(128, dtype=ml_dtypes.bfloat16),
    }
    in_maps = []
    for c in range(8):
        b, h = c // 2, c % 2
        m = dict(common)
        m["x_own"] = np.ascontiguousarray(x[b, h * NOWN:(h + 1) * NOWN])
        m["x_oth"] = np.ascontiguousarray(x[b, (1 - h) * NOWN:(2 - h) * NOWN])
        in_maps.append(m)
    return in_maps


def kernel(**inputs):
    in_maps = host_prep(inputs)
    nc = _get_nc(debug=False)
    res = run_bass_kernel_spmd(nc, in_maps, core_ids=list(range(8)))
    out = np.empty((4, 4096, 1024), np.float32)
    for c in range(8):
        b, h = c // 2, c % 2
        out[b, h * NOWN:(h + 1) * NOWN] = np.asarray(res.results[c]["out"])
    return out
